# version 03
# baseline (speedup 1.0000x reference)
import numpy as np
import ml_dtypes
import concourse.bass as bass
import concourse.mybir as mybir
from concourse.bass_utils import run_bass_kernel_spmd
from contextlib import ExitStack

F32 = mybir.dt.float32
BF16 = mybir.dt.bfloat16
I32 = mybir.dt.int32
I8 = mybir.dt.int8
AF = mybir.ActivationFunctionType
ALU = mybir.AluOpType
AX = mybir.AxisListType
BF = ml_dtypes.bfloat16

ENGS = ("tensor", "vector", "scalar", "gpsimd", "sync")


class Sched:
    def __init__(self):
        self.items = {e: [] for e in ENGS}
        self.cnt = {e: 0 for e in ENGS}
        self.pend = {e: {} for e in ENGS}

    def tok(self, e):
        return (e, self.cnt[e])

    def add(self, e, fn, deps=(), inc=1):
        p = self.pend[e]
        for d in deps:
            if d is None:
                continue
            de, dv = d
            if dv > 0 and p.get(de, 0) < dv:
                p[de] = dv
        waits = self.pend[e]
        self.pend[e] = {}
        self.cnt[e] += inc
        self.items[e].append((waits, fn, inc))
        return (e, self.cnt[e])

    def dma(self, q, out, in_, deps=(), **kw):
        return self.add(q, lambda eng: eng.dma_start(out=out, in_=in_, **kw),
                        deps=deps, inc=16)

    def barrier(self):
        toks = [self.tok(e) for e in ENGS]
        for e in ENGS:
            p = self.pend[e]
            for (de, dv) in toks:
                if dv > 0 and p.get(de, 0) < dv:
                    p[de] = dv

    def finish(self):
        self.barrier()
        for e in ENGS:
            if self.pend[e]:
                self.add(e, None, inc=0)

    def build(self, nc):
        with ExitStack() as st:
            sems = {e: st.enter_context(nc.semaphore(name="sem_" + e))
                    for e in ENGS}
            block = st.enter_context(nc.Block())
            for e in ENGS:
                items = self.items[e]

                def mk(e, items):
                    def body(eng):
                        for waits, fn, inc in items:
                            for de, dv in waits.items():
                                eng.wait_ge(sems[de], dv)
                            if fn is not None:
                                inst = fn(eng)
                                if inc:
                                    inst.then_inc(sems[e], inc)
                    return body
                getattr(block, e)(mk(e, items))


def bf(x):
    return np.ascontiguousarray(x.astype(BF) if x.dtype != BF else x)


def build_W():
    nc = bass.Bass("TRN2", target_bir_lowering=False)
    s = Sched()
    st = ExitStack()
    conv = [
        ("wg0", 256, 8192, I8), ("wu0", 256, 8192, I8), ("wdn0", 1024, 2048, I8),
        ("wdk", 256, 512, F32), ("wdv", 256, 512, F32),
        ("wuk", 64, 2048, F32), ("wuv", 64, 2048, F32),
        ("wq0", 256, 2048, F32), ("wo0", 256, 2048, F32),
        ("wq1", 256, 2048, F32), ("wk1", 256, 2048, F32),
        ("wv1", 256, 2048, F32), ("wo1", 256, 2048, F32),
    ]
    ins = {}
    outs = {}
    for name, r, c, dt in conv:
        ins[name] = nc.dram_tensor("in_" + name, [r, c], dt, kind="ExternalInput").ap()
        outs[name] = nc.dram_tensor("out_" + name, [r, c], BF16, kind="ExternalOutput").ap()
    deq = [("g1", 2048, 1024), ("u1", 2048, 1024), ("d1", 256, 8192)]
    for name, r, pc in deq:
        ins[name] = nc.dram_tensor("in_" + name, [r, pc], I32, kind="ExternalInput").ap()
        ins[name + "s"] = nc.dram_tensor("in_" + name + "s", [r, pc // 16], F32,
                                         kind="ExternalInput").ap()
        outs[name + "lo"] = nc.dram_tensor("out_" + name + "lo", [r, pc], BF16,
                                           kind="ExternalOutput").ap()
        outs[name + "hi"] = nc.dram_tensor("out_" + name + "hi", [r, pc], BF16,
                                           kind="ExternalOutput").ap()
    relp = nc.dram_tensor("relp", [1023, 64], F32, kind="ExternalInput").ap()
    mbig = nc.dram_tensor("mbig", [8191], F32, kind="ExternalOutput").ap()

    WD = 2048
    t_in = st.enter_context(nc.sbuf_tensor("t_in", [128, 8192], F32))
    t_i8 = st.enter_context(nc.sbuf_tensor("t_i8", [128, 8192], I8))
    t_out = st.enter_context(nc.sbuf_tensor("t_out", [128, 8192], BF16))
    pk = st.enter_context(nc.sbuf_tensor("pk", [128, WD], I32))
    nib = st.enter_context(nc.sbuf_tensor("nib", [128, WD], I32))
    i7 = st.enter_context(nc.sbuf_tensor("i7", [128, WD], I32))
    bits = st.enter_context(nc.sbuf_tensor("bits", [128, WD], I32))
    fa = st.enter_context(nc.sbuf_tensor("fa", [128, WD], F32))
    fb = st.enter_context(nc.sbuf_tensor("fb", [128, WD], F32))
    fc = st.enter_context(nc.sbuf_tensor("fc", [128, WD], F32))
    ob = st.enter_context(nc.sbuf_tensor("ob", [128, WD], BF16))
    sc = st.enter_context(nc.sbuf_tensor("sc", [128, 512], F32))
    mcol = st.enter_context(nc.sbuf_tensor("mcol", [128, 8], F32))
    erow = st.enter_context(nc.sbuf_tensor("erow", [1, 3584], F32))
    m0t = st.enter_context(nc.sbuf_tensor("m0t", [1, 1], F32))
    mtmp = nc.dram_tensor("mtmp", [1], F32, kind="Internal").ap()

    for name, r, c, dt in conv:
        i_ap, o_ap = ins[name], outs[name]
        for r0 in range(0, r, 128):
            p = min(128, r - r0)
            buf = t_i8 if dt == I8 else t_in
            d = s.dma("sync", buf[:p, :c], i_ap[r0:r0 + p, :])
            v = s.add("vector", lambda eng, o=t_out[:p, :c], ii=buf[:p, :c]:
                      eng.tensor_copy(out=o, in_=ii), deps=[d])
            s.dma("gpsimd", o_ap[r0:r0 + p, :], t_out[:p, :c], deps=[v])
            s.barrier()

    def dequant_tile(pk_ap, sc_ap, lo_ap, hi_ap, p, w):
        dd = s.dma("sync", pk[:p, :w], pk_ap)
        ds = s.dma("sync", sc[:p, :w // 16], sc_ap)
        for hi in (False, True):
            if hi:
                t1 = s.add("vector", lambda eng: eng.tensor_scalar(
                    out=nib[:p, :w], in0=pk[:p, :w], scalar1=4, scalar2=15,
                    op0=ALU.logical_shift_right, op1=ALU.bitwise_and), deps=[dd])
            else:
                t1 = s.add("vector", lambda eng: eng.tensor_scalar(
                    out=nib[:p, :w], in0=pk[:p, :w], scalar1=15, scalar2=None,
                    op0=ALU.bitwise_and), deps=[dd])
            t2 = s.add("vector", lambda eng: eng.tensor_scalar(
                out=i7[:p, :w], in0=nib[:p, :w], scalar1=7, scalar2=None,
                op0=ALU.bitwise_and), deps=[t1])
            t3a = s.add("vector", lambda eng: eng.tensor_scalar(
                out=bits[:p, :w], in0=i7[:p, :w], scalar1=252, scalar2=None,
                op0=ALU.add), deps=[t2])
            t3 = s.add("vector", lambda eng: eng.tensor_scalar(
                out=bits[:p, :w], in0=bits[:p, :w], scalar1=22, scalar2=None,
                op0=ALU.logical_shift_left), deps=[t3a])
            t4 = s.add("vector", lambda eng: eng.tensor_copy(
                out=fa[:p, :w], in_=i7[:p, :w]), deps=[t2])
            t5 = s.add("vector", lambda eng: eng.tensor_scalar(
                out=fb[:p, :w], in0=fa[:p, :w], scalar1=-0.25, scalar2=0.5,
                op0=ALU.mult, op1=ALU.add), deps=[t4])
            t6 = s.add("vector", lambda eng: eng.tensor_scalar(
                out=fb[:p, :w], in0=fb[:p, :w], scalar1=0.0, scalar2=None,
                op0=ALU.max), deps=[t5])
            t7 = s.add("vector", lambda eng: eng.tensor_tensor(
                out=fb[:p, :w], in0=bits[:p, :w].bitcast(F32), in1=fb[:p, :w],
                op=ALU.subtract), deps=[t3, t6])
            t8 = s.add("vector", lambda eng: eng.tensor_scalar(
                out=nib[:p, :w], in0=nib[:p, :w], scalar1=8, scalar2=None,
                op0=ALU.bitwise_and), deps=[t2, t5])
            t9 = s.add("vector", lambda eng: eng.tensor_copy(
                out=fa[:p, :w], in_=nib[:p, :w]), deps=[t8, t5])
            t10 = s.add("vector", lambda eng: eng.tensor_scalar(
                out=fa[:p, :w], in0=fa[:p, :w], scalar1=-0.25, scalar2=1.0,
                op0=ALU.mult, op1=ALU.add), deps=[t9])
            t11 = s.add("vector", lambda eng: eng.tensor_tensor(
                out=fc[:p, :w], in0=fb[:p, :w], in1=fa[:p, :w],
                op=ALU.mult), deps=[t7, t10])
            last = t11
            for b in range(w // 16):
                last = s.add("vector", lambda eng, b=b: eng.tensor_scalar(
                    out=ob[:p, b * 16:(b + 1) * 16], in0=fc[:p, b * 16:(b + 1) * 16],
                    scalar1=sc[:p, b:b + 1], scalar2=None, op0=ALU.mult),
                    deps=[t11, ds])
            s.dma("gpsimd", hi_ap if hi else lo_ap, ob[:p, :w], deps=[last])
            s.barrier()

    for name, r, pc in deq:
        for r0 in range(0, r, 128):
            p = min(128, r - r0)
            for c0 in range(0, pc, WD):
                w = min(WD, pc - c0)
                dequant_tile(ins[name][r0:r0 + p, c0:c0 + w],
                             ins[name + "s"][r0:r0 + p, c0 // 16:(c0 + w) // 16],
                             outs[name + "lo"][r0:r0 + p, c0:c0 + w],
                             outs[name + "hi"][r0:r0 + p, c0:c0 + w], p, w)

    mdeps = []
    for i in range(8):
        r0 = i * 128
        p = min(128, 1023 - r0)
        d = s.dma("sync", t_in[:p, :64], relp[r0:r0 + p, :])
        v1 = s.add("vector", lambda eng, p=p, i=i: eng.tensor_reduce(
            out=mcol[:p, i:i + 1], in_=t_in[:p, :64], axis=AX.X, op=ALU.add),
            deps=[d])
        v2 = s.add("vector", lambda eng, p=p, i=i: eng.tensor_scalar(
            out=mcol[:p, i:i + 1], in0=mcol[:p, i:i + 1], scalar1=1.0 / 64,
            scalar2=None, op0=ALU.mult), deps=[v1])
        mdeps.append(s.dma("gpsimd", mbig[3584 + r0:3584 + r0 + p],
                           mcol[:p, i:i + 1], deps=[v2]))
        s.barrier()
    z = s.add("vector", lambda eng: eng.memset(erow[:, :], 0.0))
    h1 = s.add("vector", lambda eng: eng.tensor_scalar(
        out=erow[:, :], in0=erow[:, :], scalar1=mcol[0:1, 0:1], scalar2=None,
        op0=ALU.add), deps=[z] + mdeps)
    s.dma("gpsimd", mbig[0:3584], erow[:, :], deps=[h1])
    b1 = s.dma("gpsimd", mtmp[0:1], mcol[126:127, 7:8], deps=mdeps)
    b2 = s.dma("sync", m0t[:, :], mtmp[0:1], deps=[b1])
    z2 = s.add("vector", lambda eng: eng.memset(erow[:, :], 0.0), deps=[h1])
    h2 = s.add("vector", lambda eng: eng.tensor_scalar(
        out=erow[:, :], in0=erow[:, :], scalar1=m0t[0:1, 0:1], scalar2=None,
        op0=ALU.add), deps=[z2, b2])
    s.dma("gpsimd", mbig[4607:8191], erow[:, :], deps=[h2])
    s.finish()
    s.build(nc)
    st.close()
    return nc


def build_A():
    nc = bass.Bass("TRN2", target_bir_lowering=False)
    s = Sched()
    st = ExitStack()
    xT = nc.dram_tensor("xT", [2048, 2048], F32, kind="ExternalInput").ap()
    wdk = nc.dram_tensor("wdk", [2048, 512], BF16, kind="ExternalInput").ap()
    wdv = nc.dram_tensor("wdv", [2048, 512], BF16, kind="ExternalInput").ap()
    wuk = nc.dram_tensor("wuk", [512, 2048], BF16, kind="ExternalInput").ap()
    wuv = nc.dram_tensor("wuv", [512, 2048], BF16, kind="ExternalInput").ap()
    wq0 = nc.dram_tensor("wq0", [2048, 2048], BF16, kind="ExternalInput").ap()
    wo0 = nc.dram_tensor("wo0", [2048, 2048], BF16, kind="ExternalInput").ap()
    bdk = nc.dram_tensor("bdk", [512], F32, kind="ExternalInput").ap()
    bdv = nc.dram_tensor("bdv", [512], F32, kind="ExternalInput").ap()
    buk = nc.dram_tensor("buk", [2048], F32, kind="ExternalInput").ap()
    buv = nc.dram_tensor("buv", [2048], F32, kind="ExternalInput").ap()
    bq = nc.dram_tensor("bq", [2048], F32, kind="ExternalInput").ap()
    bo = nc.dram_tensor("bo", [2048], F32, kind="ExternalInput").ap()
    wg = nc.dram_tensor("wg", [2048, 8192], BF16, kind="ExternalInput").ap()
    wu = nc.dram_tensor("wu", [2048, 8192], BF16, kind="ExternalInput").ap()
    wdn = nc.dram_tensor("wdn", [8192, 2048], BF16, kind="ExternalInput").ap()
    sg = nc.dram_tensor("sg", [8192], F32, kind="ExternalInput").ap()
    su = nc.dram_tensor("su", [8192], F32, kind="ExternalInput").ap()
    sd = nc.dram_tensor("sd", [2048], F32, kind="ExternalInput").ap()
    ident = nc.dram_tensor("ident", [128, 128], BF16, kind="ExternalInput").ap()
    y1T = nc.dram_tensor("y1T", [2048, 2048], F32, kind="ExternalOutput").ap()
    xn1T = nc.dram_tensor("xn1T", [2048, 2048], BF16, kind="ExternalOutput").ap()
    xn_scr = nc.dram_tensor("xn_scr", [2048, 2048], BF16, kind="Internal").ap()
    yp_scr = nc.dram_tensor("yp_scr", [2048, 2048], F32, kind="Internal").ap()

    sb = lambda n, sh, dt=F32: st.enter_context(nc.sbuf_tensor(n, sh, dt))
    ps = lambda n, sh, dt=F32: st.enter_context(nc.psum_tensor(n, sh, dt))
    x_t = [sb(f"x{i}", [128, 512]) for i in range(16)]
    xn_b = [sb(f"xn{i}", [128, 512], BF16) for i in range(16)]
    xbf = [sb(f"xbf{i}", [128, 512], BF16) for i in range(2)]
    xsq = [sb(f"xsq{i}", [128, 512], BF16) for i in range(2)]
    ones_b = sb("ones_b", [128, 1], BF16)
    ones_f = sb("ones_f", [1, 128])
    srow = sb("srow", [1, 512])
    srow2 = sb("srow2", [1, 512])
    mrow = sb("mrow", [1, 512])
    rrow = sb("rrow", [1, 512])
    tmpf = sb("tmpf", [128, 512])
    ck_b = [sb(f"ck{i}", [128, 512], BF16) for i in range(4)]
    cv_b = [sb(f"cv{i}", [128, 512], BF16) for i in range(4)]
    wbA = [sb(f"wbA{i}", [128, 512], BF16) for i in range(16)]
    wbB = [sb(f"wbB{i}", [128, 512], BF16) for i in range(16)]
    wbE = [sb(f"wbE{i}", [128, 128], BF16) for i in range(16)]
    q_s = [sb(f"qs{i}", [128, 2048], BF16) for i in range(4)]
    k_s = [sb(f"ks{i}", [128, 2048], BF16) for i in range(4)]
    v_s = [sb(f"vs{i}", [128, 2048], BF16) for i in range(4)]
    bb_k = sb("bb_k", [128, 2048], BF16)
    bb_v = sb("bb_v", [128, 2048], BF16)
    bb_q = sb("bb_q", [128, 2048], BF16)
    bcol_k = sb("bcol_k", [128, 4])
    bcol_v = sb("bcol_v", [128, 4])
    bcol_o = sb("bcol_o", [128, 16])
    scol_g = sb("scol_g", [128, 64])
    scol_u = sb("scol_u", [128, 64])
    scol_d = sb("scol_d", [128, 16])
    brow = sb("brow", [1, 2048])
    sim = sb("sim", [128, 256])
    esim = sb("esim", [128, 256])
    attw = sb("attw", [128, 256])
    mx = sb("mx", [128, 16])
    sm = sb("sm", [128, 16])
    rcp = sb("rcp", [128, 16])
    dummy = sb("dummy_sb", [128, 128])
    zero = sb("zero", [128, 128])
    oacc = [sb(f"oacc{i}", [128, 128]) for i in range(2)]
    ao_b = sb("ao_b", [128, 2048], BF16)
    aoT = [sb(f"aoT{i}", [128, 512], BF16) for i in range(16)]
    id_sb = sb("id_sb", [128, 128], BF16)
    g_f = sb("g_f", [128, 512])
    u_f = sb("u_f", [128, 512])

    def fuse_ap(o):
        if o < 48:
            t = (q_s + k_s + v_s)[o // 4]
            return t[:, (o % 4) * 512:(o % 4 + 1) * 512]
        return aoT[o - 48][:, :]

    def wbF_ap(o):
        return wbA[o // 4][:, (o % 4) * 128:(o % 4 + 1) * 128]

    psA = ps("psA", [128, 512])
    psB = ps("psB", [128, 512])
    psC = ps("psC", [128, 512])
    psD = ps("psD", [128, 512])
    psE = ps("psE", [128, 512])
    psF = ps("psF", [128, 512])
    psG = ps("psG", [128, 512])
    psT = ps("psT", [128, 128], BF16)

    s.add("vector", lambda eng: eng.memset(ones_b[:, :], 1.0))
    s.add("vector", lambda eng: eng.memset(ones_f[:, :], 1.0))
    s.add("vector", lambda eng: eng.memset(zero[:, :], 0.0))
    s.dma("sync", id_sb[:, :], ident[:, :])
    s.dma("sync", bcol_k[:, :], bdk.rearrange("(c p) -> p c", p=128),
          allow_slow_non_contiguous=True)
    s.dma("sync", bcol_v[:, :], bdv.rearrange("(c p) -> p c", p=128),
          allow_slow_non_contiguous=True)
    s.dma("sync", bcol_o[:, :], bo.rearrange("(c p) -> p c", p=128),
          allow_slow_non_contiguous=True)
    s.dma("sync", scol_g[:, :], sg.rearrange("(c p) -> p c", p=128),
          allow_slow_non_contiguous=True)
    s.dma("sync", scol_u[:, :], su.rearrange("(c p) -> p c", p=128),
          allow_slow_non_contiguous=True)
    s.dma("sync", scol_d[:, :], sd.rearrange("(c p) -> p c", p=128),
          allow_slow_non_contiguous=True)
    s.barrier()
    for row_ap, dst in ((buk, bb_k), (buv, bb_v), (bq, bb_q)):
        d = s.dma("sync", brow[:, :], row_ap[None, :])
        s.barrier()
        for j in range(4):
            m = s.add("tensor", lambda eng, j=j: eng.matmul(
                out=psG[:, :], lhsT=ones_f[:, :], rhs=brow[:, j * 512:(j + 1) * 512],
                start=True, stop=True), deps=[d])
            s.add("vector", lambda eng, j=j, dst=dst: eng.tensor_copy(
                out=dst[:, j * 512:(j + 1) * 512], in_=psG[:, :]), deps=[m])
            s.barrier()

    def layernorm(src_tiles, out_tiles, extra_dep=()):
        for f in range(16):
            b = f % 2
            c1 = s.add("vector", lambda eng, f=f, b=b: eng.tensor_copy(
                out=xbf[b][:, :], in_=src_tiles[f][:, :]), deps=list(extra_dep))
            m1 = s.add("tensor", lambda eng, f=f, b=b: eng.matmul(
                out=psE[0:1, :], lhsT=ones_b[:, :], rhs=xbf[b][:, :],
                start=(f == 0), stop=(f == 15)), deps=[c1])
            q1 = s.add("scalar", lambda eng, f=f, b=b: eng.activation(
                out=xsq[b][:, :], in_=src_tiles[f][:, :], func=AF.Square),
                deps=list(extra_dep))
            m2 = s.add("tensor", lambda eng, f=f, b=b: eng.matmul(
                out=psF[0:1, :], lhsT=ones_b[:, :], rhs=xsq[b][:, :],
                start=(f == 0), stop=(f == 15)), deps=[q1])
            s.barrier()
        v1 = s.add("vector", lambda eng: eng.tensor_scalar(
            out=mrow[:, :], in0=psE[0:1, :], scalar1=1.0 / 2048, scalar2=None,
            op0=ALU.mult))
        v2 = s.add("vector", lambda eng: eng.tensor_scalar(
            out=srow[:, :], in0=psF[0:1, :], scalar1=1.0 / 2048, scalar2=None,
            op0=ALU.mult))
        v3 = s.add("vector", lambda eng: eng.tensor_tensor(
            out=srow2[:, :], in0=mrow[:, :], in1=mrow[:, :], op=ALU.mult),
            deps=[v1])
        v4 = s.add("vector", lambda eng: eng.tensor_tensor(
            out=srow[:, :], in0=srow[:, :], in1=srow2[:, :], op=ALU.subtract),
            deps=[v2, v3])
        v4b = s.add("vector", lambda eng: eng.tensor_scalar(
            out=srow[:, :], in0=srow[:, :], scalar1=1e-5, scalar2=None,
            op0=ALU.add), deps=[v4])
        a1 = s.add("scalar", lambda eng: eng.activation(
            out=srow2[:, :], in_=srow[:, :], func=AF.Sqrt), deps=[v4b])
        v5 = s.add("vector", lambda eng: eng.reciprocal(
            out=rrow[:, :], in_=srow2[:, :]), deps=[a1])
        mB = s.add("tensor", lambda eng: eng.matmul(
            out=psG[:, :], lhsT=ones_f[:, :], rhs=mrow[:, :], start=True,
            stop=True), deps=[v1])
        mR = s.add("tensor", lambda eng: eng.matmul(
            out=psA[:, :], lhsT=ones_f[:, :], rhs=rrow[:, :], start=True,
            stop=True), deps=[v5])
        s.barrier()
        for f in range(16):
            w1 = s.add("vector", lambda eng, f=f: eng.tensor_tensor(
                out=tmpf[:, :], in0=src_tiles[f][:, :], in1=psG[:, :],
                op=ALU.subtract))
            w2 = s.add("vector", lambda eng, f=f: eng.tensor_tensor(
                out=out_tiles[f][:, :], in0=tmpf[:, :], in1=psA[:, :],
                op=ALU.mult), deps=[w1])
            s.barrier()

    for tt in range(4):
        t0 = tt * 512
        for f in range(16):
            s.dma("sync", x_t[f][:, :], xT[f * 128:(f + 1) * 128, t0:t0 + 512])
        s.barrier()
        layernorm(x_t, xn_b)
        for f in range(16):
            s.dma("gpsimd", xn_scr[f * 128:(f + 1) * 128, t0:t0 + 512],
                  xn_b[f][:, :])
        for f in range(16):
            s.dma("sync", wbA[f][:, :], wdk[f * 128:(f + 1) * 128, :])
            s.dma("sync", wbB[f][:, :], wdv[f * 128:(f + 1) * 128, :])
        s.barrier()
        for lat in range(4):
            m1 = s.add("tensor", lambda eng, lat=lat: [eng.matmul(
                out=psB[:, :], lhsT=wbA[f][:, lat * 128:(lat + 1) * 128],
                rhs=xn_b[f][:, :], start=(f == 0), stop=(f == 15))
                for f in range(16)][-1])
            s.add("vector", lambda eng, lat=lat: eng.tensor_scalar(
                out=ck_b[lat][:, :], in0=psB[:, :], scalar1=bcol_k[:, lat:lat + 1],
                scalar2=None, op0=ALU.add), deps=[m1])
            m2 = s.add("tensor", lambda eng, lat=lat: [eng.matmul(
                out=psC[:, :], lhsT=wbB[f][:, lat * 128:(lat + 1) * 128],
                rhs=xn_b[f][:, :], start=(f == 0), stop=(f == 15))
                for f in range(16)][-1])
            s.add("vector", lambda eng, lat=lat: eng.tensor_scalar(
                out=cv_b[lat][:, :], in0=psC[:, :], scalar1=bcol_v[:, lat:lat + 1],
                scalar2=None, op0=ALU.add), deps=[m2])
            s.barrier()
        for hd in range(4):
            for f in range(16):
                s.dma("sync", wbB[f][:, :],
                      wq0[f * 128:(f + 1) * 128, hd * 512:(hd + 1) * 512])
            s.barrier()
            for ts4 in range(4):
                m = s.add("tensor", lambda eng, ts4=ts4, hd=hd: [eng.matmul(
                    out=psA[:, :], lhsT=xn_b[f][:, ts4 * 128:(ts4 + 1) * 128],
                    rhs=wbB[f][:, :], start=(f == 0), stop=(f == 15))
                    for f in range(16)][-1])
                s.add("vector", lambda eng, ts4=ts4, hd=hd: eng.tensor_tensor(
                    out=q_s[ts4][:, hd * 512:(hd + 1) * 512], in0=psA[:, :],
                    in1=bb_q[:, hd * 512:(hd + 1) * 512], op=ALU.add), deps=[m])
                s.barrier()
        for lat in range(4):
            for hd in range(4):
                s.dma("sync", wbB[lat * 4 + hd][:, :],
                      wuk[lat * 128:(lat + 1) * 128, hd * 512:(hd + 1) * 512])
        s.barrier()
        for ts4 in range(4):
            for hd in range(4):
                m = s.add("tensor", lambda eng, ts4=ts4, hd=hd: [eng.matmul(
                    out=psA[:, :], lhsT=ck_b[lat][:, ts4 * 128:(ts4 + 1) * 128],
                    rhs=wbB[lat * 4 + hd][:, :], start=(lat == 0),
                    stop=(lat == 3)) for lat in range(4)][-1])
                s.add("vector", lambda eng, ts4=ts4, hd=hd: eng.tensor_tensor(
                    out=k_s[ts4][:, hd * 512:(hd + 1) * 512], in0=psA[:, :],
                    in1=bb_k[:, hd * 512:(hd + 1) * 512], op=ALU.add), deps=[m])
                s.barrier()
        for lat in range(4):
            for hd in range(4):
                s.dma("sync", wbB[lat * 4 + hd][:, :],
                      wuv[lat * 128:(lat + 1) * 128, hd * 512:(hd + 1) * 512])
        s.barrier()
        for ts4 in range(4):
            for hd in range(4):
                m = s.add("tensor", lambda eng, ts4=ts4, hd=hd: [eng.matmul(
                    out=psA[:, :], lhsT=cv_b[lat][:, ts4 * 128:(ts4 + 1) * 128],
                    rhs=wbB[lat * 4 + hd][:, :], start=(lat == 0),
                    stop=(lat == 3)) for lat in range(4)][-1])
                s.add("vector", lambda eng, ts4=ts4, hd=hd: eng.tensor_tensor(
                    out=v_s[ts4][:, hd * 512:(hd + 1) * 512], in0=psA[:, :],
                    in1=bb_v[:, hd * 512:(hd + 1) * 512], op=ALU.add), deps=[m])
                s.barrier()
        for ts4 in range(4):
            for h in range(16):
                for g in range(16):
                    s.add("vector", lambda eng, ts4=ts4, h=h, g=g:
                          eng.scalar_tensor_tensor(
                              out=dummy[:, :], in0=q_s[ts4][:, h * 128:(h + 1) * 128],
                              scalar=0.08838834764831845,
                              in1=k_s[ts4][:, g * 128:(g + 1) * 128],
                              op0=ALU.mult, op1=ALU.mult,
                              accum_out=sim[:, h * 16 + g:h * 16 + g + 1]))
            s.barrier()
            for h in range(16):
                r1 = s.add("vector", lambda eng, h=h: eng.tensor_reduce(
                    out=mx[:, h:h + 1], in_=sim[:, h * 16:(h + 1) * 16], axis=AX.X,
                    op=ALU.max, negate=True))
                r2 = s.add("scalar", lambda eng, h=h: eng.activation(
                    out=esim[:, h * 16:(h + 1) * 16], in_=sim[:, h * 16:(h + 1) * 16],
                    func=AF.Exp, bias=mx[:, h:h + 1]), deps=[r1])
                r3 = s.add("vector", lambda eng, h=h: eng.tensor_reduce(
                    out=sm[:, h:h + 1], in_=esim[:, h * 16:(h + 1) * 16], axis=AX.X,
                    op=ALU.add), deps=[r2])
                r4 = s.add("vector", lambda eng, h=h: eng.reciprocal(
                    out=rcp[:, h:h + 1], in_=sm[:, h:h + 1]), deps=[r3])
                s.add("vector", lambda eng, h=h: eng.tensor_scalar(
                    out=attw[:, h * 16:(h + 1) * 16], in0=esim[:, h * 16:(h + 1) * 16],
                    scalar1=rcp[:, h:h + 1], scalar2=None, op0=ALU.mult),
                    deps=[r4])
            s.barrier()
            for h in range(16):
                prev = None
                for g in range(16):
                    prev = s.add("vector", lambda eng, ts4=ts4, h=h, g=g:
                                 eng.scalar_tensor_tensor(
                                     out=oacc[g % 2][:, :],
                                     in0=v_s[ts4][:, g * 128:(g + 1) * 128],
                                     scalar=attw[:, h * 16 + g:h * 16 + g + 1],
                                     in1=(zero[:, :] if g == 0 else
                                          oacc[(g + 1) % 2][:, :]),
                                     op0=ALU.mult, op1=ALU.add),
                                 deps=[prev] if prev else [])
                s.add("vector", lambda eng, h=h: eng.tensor_copy(
                    out=ao_b[:, h * 128:(h + 1) * 128], in_=oacc[1][:, :]),
                    deps=[prev])
            s.barrier()
            for f in range(16):
                tr = s.add("tensor", lambda eng, f=f: eng.transpose(
                    out=psT[:, :], in_=ao_b[:, f * 128:(f + 1) * 128],
                    identity=id_sb[:, :]))
                s.add("vector", lambda eng, f=f, ts4=ts4: eng.tensor_copy(
                    out=aoT[f][:, ts4 * 128:(ts4 + 1) * 128], in_=psT[:, :]),
                    deps=[tr])
                s.barrier()
        for fo in range(16):
            for hd in range(16):
                s.dma("sync", wbE[hd][:, :],
                      wo0[hd * 128:(hd + 1) * 128, fo * 128:(fo + 1) * 128])
            s.barrier()
            m = s.add("tensor", lambda eng, fo=fo: [eng.matmul(
                out=psD[:, :], lhsT=wbE[hd][:, :], rhs=aoT[hd][:, :],
                start=(hd == 0), stop=(hd == 15)) for hd in range(16)][-1])
            s.add("vector", lambda eng, fo=fo: eng.scalar_tensor_tensor(
                out=x_t[fo][:, :], in0=psD[:, :], scalar=bcol_o[:, fo:fo + 1],
                in1=x_t[fo][:, :], op0=ALU.add, op1=ALU.add), deps=[m])
            s.barrier()
        for f in range(16):
            s.dma("gpsimd", yp_scr[f * 128:(f + 1) * 128, t0:t0 + 512],
                  x_t[f][:, :])
        s.barrier()

    for tt in range(4):
        t0 = tt * 512
        for f in range(16):
            s.dma("sync", x_t[f][:, :], yp_scr[f * 128:(f + 1) * 128, t0:t0 + 512])
            s.dma("sync", xn_b[f][:, :], xn_scr[f * 128:(f + 1) * 128, t0:t0 + 512])
        s.barrier()
        for o in range(64):
            for f in range(16):
                s.dma("sync", wbA[f][:, 0:128],
                      wg[f * 128:(f + 1) * 128, o * 128:(o + 1) * 128])
                s.dma("sync", wbB[f][:, 0:128],
                      wu[f * 128:(f + 1) * 128, o * 128:(o + 1) * 128])
            s.barrier()
            m1 = s.add("tensor", lambda eng, o=o: [eng.matmul(
                out=psA[:, :], lhsT=wbA[f][:, 0:128], rhs=xn_b[f][:, :],
                start=(f == 0), stop=(f == 15)) for f in range(16)][-1])
            m2 = s.add("tensor", lambda eng, o=o: [eng.matmul(
                out=psB[:, :], lhsT=wbB[f][:, 0:128], rhs=xn_b[f][:, :],
                start=(f == 0), stop=(f == 15)) for f in range(16)][-1])
            v1 = s.add("vector", lambda eng, o=o: eng.tensor_scalar(
                out=g_f[:, :], in0=psA[:, :], scalar1=scol_g[:, o:o + 1],
                scalar2=None, op0=ALU.mult), deps=[m1])
            a1 = s.add("scalar", lambda eng, o=o: eng.activation(
                out=g_f[:, :], in_=g_f[:, :], func=AF.Silu), deps=[v1])
            v2 = s.add("vector", lambda eng, o=o: eng.tensor_scalar(
                out=u_f[:, :], in0=psB[:, :], scalar1=scol_u[:, o:o + 1],
                scalar2=None, op0=ALU.mult), deps=[m2])
            s.add("vector", lambda eng, o=o: eng.tensor_tensor(
                out=fuse_ap(o), in0=g_f[:, :], in1=u_f[:, :], op=ALU.mult),
                deps=[a1, v2])
            s.barrier()
        for f in range(16):
            for o in range(64):
                s.dma("sync", wbF_ap(o),
                      wdn[o * 128:(o + 1) * 128, f * 128:(f + 1) * 128])
            s.barrier()
            pd = psC if f % 2 == 0 else psD
            m = s.add("tensor", lambda eng, f=f, pd=pd: [eng.matmul(
                out=pd[:, :], lhsT=wbF_ap(o), rhs=fuse_ap(o),
                start=(o == 0), stop=(o == 63)) for o in range(64)][-1])
            s.add("vector", lambda eng, f=f, pd=pd: eng.scalar_tensor_tensor(
                out=x_t[f][:, :], in0=pd[:, :], scalar=scol_d[:, f:f + 1],
                in1=x_t[f][:, :], op0=ALU.mult, op1=ALU.add), deps=[m])
            s.barrier()
        layernorm(x_t, xn_b)
        for f in range(16):
            s.dma("gpsimd", y1T[f * 128:(f + 1) * 128, t0:t0 + 512], x_t[f][:, :])
            s.dma("gpsimd", xn1T[f * 128:(f + 1) * 128, t0:t0 + 512],
                  xn_b[f][:, :])
        s.barrier()
    s.finish()
    s.build(nc)
    st.close()
    return nc


def build_B():
    nc = bass.Bass("TRN2", target_bir_lowering=False)
    s = Sched()
    st = ExitStack()
    xq = nc.dram_tensor("xq", [2048, 2048], BF16, kind="ExternalInput").ap()
    xlo = nc.dram_tensor("xlo", [2048, 2048], BF16, kind="ExternalInput").ap()
    xhi = nc.dram_tensor("xhi", [2048, 2048], BF16, kind="ExternalInput").ap()
    wq1 = nc.dram_tensor("wq1", [2048, 2048], BF16, kind="ExternalInput").ap()
    wk1 = nc.dram_tensor("wk1", [2048, 2048], BF16, kind="ExternalInput").ap()
    wv1 = nc.dram_tensor("wv1", [2048, 2048], BF16, kind="ExternalInput").ap()
    wo1 = nc.dram_tensor("wo1", [2048, 2048], BF16, kind="ExternalInput").ap()
    m2 = nc.dram_tensor("m2", [128, 6016], F32, kind="ExternalInput").ap()
    aT = nc.dram_tensor("aT", [2048, 2048], F32, kind="ExternalOutput").ap()
    kT_scr = nc.dram_tensor("kT_scr", [2048, 4096], BF16, kind="Internal").ap()
    q_scr = nc.dram_tensor("q_scr", [2048, 2048], BF16, kind="Internal").ap()
    v_scr = nc.dram_tensor("v_scr", [4096, 2080], BF16, kind="Internal").ap()

    sb = lambda n, sh, dt=F32: st.enter_context(nc.sbuf_tensor(n, sh, dt))
    ps = lambda n, sh, dt=F32: st.enter_context(nc.psum_tensor(n, sh, dt))
    xt = [sb(f"xt{i}", [128, 512], BF16) for i in range(16)]
    wb = [sb(f"wb{i}", [128, 512], BF16) for i in range(16)]
    stage = sb("stage", [128, 512], BF16)
    vstage = sb("vstage", [128, 520], BF16)
    btile = sb("btile", [128, 512])
    biasT = [sb(f"bT{i}", [128, 512], BF16) for i in range(32)]
    kgrp = [sb(f"kg{i}", [128, 4096], BF16) for i in range(2)]
    vgrp = sb("vgrp", [128, 8320], BF16)
    qgrp = [sb(f"qg{i}", [128, 512], BF16) for i in range(2)]
    st_f = sb("st_f", [128, 1024])
    pt_b = sb("pt_b", [128, 1024], BF16)
    ctxT = [sb(f"cT{i}", [128, 512], BF16) for i in range(16)]
    rs_b = sb("rs_b", [128, 512])
    ones_q = sb("ones_q", [128, 128])
    osb = sb("osb", [128, 512])
    psP = ps("psP", [128, 512])
    psS = ps("psS", [128, 1024])
    psC0 = ps("psC0", [65, 512])
    psC1 = ps("psC1", [65, 512])
    psBC = ps("psBC", [64, 512])
    psO = ps("psO", [128, 512])

    s.add("vector", lambda eng: eng.memset(ones_q[:, :], 1.0))
    s.barrier()

    xin = {0: xlo, 1: xhi}

    for which, wmat, src_list, scr, scale in (
            ("q", wq1, [(0, xq)], q_scr, 0.125),
            ("k", wk1, [(0, xlo), (1, xhi)], kT_scr, None)):
        for half, xsrc in src_list:
            for tq in range(4):
                for f in range(16):
                    s.dma("sync", xt[f][:, :],
                          xsrc[f * 128:(f + 1) * 128, tq * 512:(tq + 1) * 512])
                s.barrier()
                for hd in range(16):
                    for f in range(16):
                        s.dma("sync", wb[f][:, 0:128],
                              wmat[f * 128:(f + 1) * 128, hd * 128:(hd + 1) * 128])
                    s.barrier()
                    m = s.add("tensor", lambda eng: [eng.matmul(
                        out=psP[:, :], lhsT=wb[f][:, 0:128], rhs=xt[f][:, :],
                        start=(f == 0), stop=(f == 15)) for f in range(16)][-1])
                    if scale is not None:
                        v = s.add("vector", lambda eng, sc=scale: eng.tensor_scalar(
                            out=stage[:, :], in0=psP[:, :], scalar1=sc,
                            scalar2=None, op0=ALU.mult), deps=[m])
                    else:
                        v = s.add("vector", lambda eng: eng.tensor_copy(
                            out=stage[:, :], in_=psP[:, :]), deps=[m])
                    c0 = half * 2048 + tq * 512
                    s.dma("gpsimd", scr[hd * 128:(hd + 1) * 128, c0:c0 + 512],
                          stage[:, :], deps=[v])
                    s.barrier()
    for half, xsrc in ((0, xlo), (1, xhi)):
        for hs in range(4):
            for f in range(16):
                s.dma("sync", wb[f][:, :],
                      wv1[f * 128:(f + 1) * 128, hs * 512:(hs + 1) * 512])
            s.barrier()
            for tk in range(16):
                for f in range(16):
                    s.dma("sync", xt[f][:, 0:128],
                          xsrc[f * 128:(f + 1) * 128, tk * 128:(tk + 1) * 128])
                s.barrier()
                m = s.add("tensor", lambda eng: [eng.matmul(
                    out=psP[:, :], lhsT=xt[f][:, 0:128], rhs=wb[f][:, :],
                    start=(f == 0), stop=(f == 15)) for f in range(16)][-1])
                z = s.add("vector", lambda eng: eng.memset(vstage[:, :], 1.0),
                          deps=[m])
                last = z
                for h in range(8):
                    last = s.add("vector", lambda eng, h=h: eng.tensor_copy(
                        out=vstage[:, h * 65:h * 65 + 64],
                        in_=psP[:, h * 64:(h + 1) * 64]), deps=[z])
                s.dma("gpsimd",
                      v_scr[half * 2048 + tk * 128:half * 2048 + (tk + 1) * 128,
                            hs * 520:(hs + 1) * 520],
                      vstage[:, :], deps=[last])
                s.barrier()

    for tq in range(4):
        t0 = tq * 512
        for tkc in range(32):
            vb = 3968 - tkc * 128 + t0
            d = s.dma("sync", btile[:, :], m2[0:128, vb:vb + 512])
            s.add("vector", lambda eng, tkc=tkc: eng.tensor_copy(
                out=biasT[tkc][:, :], in_=btile[:, :]), deps=[d])
            s.barrier()
        for grp in range(8):
            d1 = s.dma("sync", kgrp[0][:, :], kT_scr[grp * 256:grp * 256 + 128, :])
            d2 = s.dma("sync", kgrp[1][:, :],
                       kT_scr[grp * 256 + 128:grp * 256 + 256, :])
            d3 = s.dma("sync", vgrp[:, :].rearrange("p (c m) -> p c m", m=260),
                       v_scr[:, grp * 260:(grp + 1) * 260].rearrange(
                           "(c p) m -> p c m", p=128))
            d4 = s.dma("sync", qgrp[0][:, :],
                       q_scr[grp * 256:grp * 256 + 128, t0:t0 + 512])
            d5 = s.dma("sync", qgrp[1][:, :],
                       q_scr[grp * 256 + 128:grp * 256 + 256, t0:t0 + 512])
            s.barrier()
            for pair in range(2):
                cps = (psC0, psC1)
                for tk in range(32):
                    m = None
                    for i in range(2):
                        hl = pair * 2 + i
                        ch, po = hl // 2, 64 * (hl % 2)
                        m = s.add("tensor", lambda eng, i=i, hl=hl, ch=ch, po=po,
                                  tk=tk: eng.matmul(
                                      out=psS[:, i * 512:(i + 1) * 512],
                                      lhsT=kgrp[ch][po:po + 64, tk * 128:(tk + 1) * 128],
                                      rhs=qgrp[ch][po:po + 64, :],
                                      start=True, stop=True),
                                  deps=[m] if m else [])
                    b1 = s.add("vector", lambda eng, tk=tk: eng.tensor_tensor(
                        out=st_f[:, 0:512], in0=psS[:, 0:512], in1=biasT[tk][:, :],
                        op=ALU.add), deps=[m])
                    b2 = s.add("vector", lambda eng, tk=tk: eng.tensor_tensor(
                        out=st_f[:, 512:1024], in0=psS[:, 512:1024],
                        in1=biasT[tk][:, :], op=ALU.add), deps=[m, b1])
                    e1 = s.add("scalar", lambda eng: eng.activation(
                        out=pt_b[:, :], in_=st_f[:, :], func=AF.Exp),
                        deps=[b1, b2])
                    for i in range(2):
                        hl = pair * 2 + i
                        s.add("tensor", lambda eng, i=i, hl=hl, tk=tk,
                              cp=cps[i]: eng.matmul(
                                  out=cp[:, :],
                                  lhsT=vgrp[:, tk * 260 + hl * 65:tk * 260 + hl * 65 + 65],
                                  rhs=pt_b[:, i * 512:(i + 1) * 512],
                                  start=(tk == 0), stop=(tk == 31)),
                              deps=[e1])
                    s.barrier()
                for i in range(2):
                    hl = pair * 2 + i
                    hglob = grp * 4 + hl
                    cp = cps[i]
                    r1 = s.add("vector", lambda eng, cp=cp: eng.reciprocal(
                        out=rs_b[64:65, :], in_=cp[64:65, :]))
                    mb = s.add("tensor", lambda eng: eng.matmul(
                        out=psBC[:, :], lhsT=ones_q[64:65, 0:64],
                        rhs=rs_b[64:65, :], start=True, stop=True), deps=[r1])
                    c1 = s.add("vector", lambda eng: eng.tensor_copy(
                        out=rs_b[0:64, :], in_=psBC[:, :]), deps=[mb])
                    ct = ctxT[hglob // 2]
                    po = 64 * (hglob % 2)
                    s.add("vector", lambda eng, cp=cp, ct=ct, po=po:
                          eng.tensor_tensor(out=ct[po:po + 64, :],
                                            in0=cp[0:64, :], in1=rs_b[0:64, :],
                                            op=ALU.mult), deps=[c1])
                    s.barrier()
        for fo in range(16):
            for hd in range(16):
                s.dma("sync", wb[hd][:, 0:128],
                      wo1[hd * 128:(hd + 1) * 128, fo * 128:(fo + 1) * 128])
            s.barrier()
            m = s.add("tensor", lambda eng, fo=fo: [eng.matmul(
                out=psO[:, :], lhsT=wb[hd][:, 0:128], rhs=ctxT[hd][:, :],
                start=(hd == 0), stop=(hd == 15)) for hd in range(16)][-1])
            v = s.add("vector", lambda eng: eng.tensor_copy(
                out=osb[:, :], in_=psO[:, :]), deps=[m])
            s.dma("gpsimd", aT[fo * 128:(fo + 1) * 128, t0:t0 + 512], osb[:, :],
                  deps=[v])
            s.barrier()
    s.finish()
    s.build(nc)
    st.close()
    return nc


def build_C():
    nc = bass.Bass("TRN2", target_bir_lowering=False)
    s = Sched()
    st = ExitStack()
    xn = nc.dram_tensor("xn", [2048, 2048], BF16, kind="ExternalInput").ap()
    y1 = nc.dram_tensor("y1", [2048, 2048], F32, kind="ExternalInput").ap()
    at = nc.dram_tensor("at", [2048, 2048], F32, kind="ExternalInput").ap()
    wgT = nc.dram_tensor("wgT", [2048, 16384], BF16, kind="ExternalInput").ap()
    wuT = nc.dram_tensor("wuT", [2048, 16384], BF16, kind="ExternalInput").ap()
    wdT = nc.dram_tensor("wdT", [16384, 2048], BF16, kind="ExternalInput").ap()
    y2 = nc.dram_tensor("y2", [2048, 2048], F32, kind="ExternalOutput").ap()

    sb = lambda n, sh, dt=F32: st.enter_context(nc.sbuf_tensor(n, sh, dt))
    ps = lambda n, sh, dt=F32: st.enter_context(nc.psum_tensor(n, sh, dt))
    xnt = [sb(f"xn{i}", [128, 1024], BF16) for i in range(16)]
    acc = [sb(f"ac{i}", [128, 1024]) for i in range(16)]
    wgb = [sb(f"wg{i}", [128, 512], BF16) for i in range(16)]
    wub = [sb(f"wu{i}", [128, 512], BF16) for i in range(16)]
    wdb = [sb(f"wd{i}", [128, 2048], BF16) for i in range(4)]
    inter = [sb(f"it{i}", [128, 512], BF16) for i in range(4)]
    gf = sb("gf", [128, 512])
    sq = sb("sq", [128, 512])
    wf = sb("wf", [128, 512])
    uf = sb("uf", [128, 512])
    th = sb("th", [128, 512])
    rt = sb("rt", [128, 1024])
    psG = ps("psG", [128, 512])
    psU = ps("psU", [128, 512])
    psD0 = ps("psD0", [128, 512])
    psD1 = ps("psD1", [128, 512])

    for half in range(2):
        h0 = half * 1024
        for f in range(16):
            s.dma("sync", xnt[f][:, :], xn[f * 128:(f + 1) * 128, h0:h0 + 1024])
            s.add("vector", lambda eng, f=f: eng.memset(acc[f][:, :], 0.0))
        s.barrier()
        for ns in range(32):
            n0 = ns * 512
            for f in range(16):
                s.dma("sync", wgb[f][:, :], wgT[f * 128:(f + 1) * 128, n0:n0 + 512])
                s.dma("sync", wub[f][:, :], wuT[f * 128:(f + 1) * 128, n0:n0 + 512])
            for j in range(4):
                s.dma("sync", wdb[j][:, :],
                      wdT[n0 + j * 128:n0 + (j + 1) * 128, :])
            s.barrier()
            for t5 in range(2):
                tt0 = t5 * 512
                for j in range(4):
                    m1 = s.add("tensor", lambda eng, j=j, tt0=tt0: [eng.matmul(
                        out=psG[:, :], lhsT=wgb[f][:, j * 128:(j + 1) * 128],
                        rhs=xnt[f][:, tt0:tt0 + 512], start=(f == 0),
                        stop=(f == 15)) for f in range(16)][-1])
                    m2 = s.add("tensor", lambda eng, j=j, tt0=tt0: [eng.matmul(
                        out=psU[:, :], lhsT=wub[f][:, j * 128:(j + 1) * 128],
                        rhs=xnt[f][:, tt0:tt0 + 512], start=(f == 0),
                        stop=(f == 15)) for f in range(16)][-1])
                    c1 = s.add("vector", lambda eng: eng.tensor_copy(
                        out=gf[:, :], in_=psG[:, :]), deps=[m1])
                    c2 = s.add("scalar", lambda eng: eng.activation(
                        out=sq[:, :], in_=gf[:, :], func=AF.Square), deps=[c1])
                    c3 = s.add("vector", lambda eng: eng.tensor_scalar(
                        out=wf[:, :], in0=sq[:, :], scalar1=0.044715, scalar2=1.0,
                        op0=ALU.mult, op1=ALU.add), deps=[c2])
                    c4 = s.add("vector", lambda eng: eng.tensor_tensor(
                        out=uf[:, :], in0=wf[:, :], in1=gf[:, :], op=ALU.mult),
                        deps=[c3])
                    c5 = s.add("scalar", lambda eng: eng.activation(
                        out=th[:, :], in_=uf[:, :], func=AF.Tanh,
                        scale=0.7978845608028654), deps=[c4])
                    c6 = s.add("vector", lambda eng: eng.tensor_scalar(
                        out=th[:, :], in0=th[:, :], scalar1=0.5, scalar2=0.5,
                        op0=ALU.mult, op1=ALU.add), deps=[c5])
                    c7 = s.add("vector", lambda eng: eng.tensor_tensor(
                        out=gf[:, :], in0=gf[:, :], in1=th[:, :], op=ALU.mult),
                        deps=[c6])
                    c8 = s.add("vector", lambda eng, j=j: eng.tensor_tensor(
                        out=inter[j][:, :], in0=gf[:, :], in1=psU[:, :],
                        op=ALU.mult), deps=[c7, m2])
                    s.barrier()
                for f in range(16):
                    pd = psD0 if f % 2 == 0 else psD1
                    m = s.add("tensor", lambda eng, f=f, pd=pd: [eng.matmul(
                        out=pd[:, :], lhsT=wdb[j][:, f * 128:(f + 1) * 128],
                        rhs=inter[j][:, :], start=(j == 0), stop=(j == 3))
                        for j in range(4)][-1])
                    s.add("vector", lambda eng, f=f, pd=pd, tt0=tt0:
                          eng.tensor_tensor(out=acc[f][:, tt0:tt0 + 512],
                                            in0=pd[:, :],
                                            in1=acc[f][:, tt0:tt0 + 512],
                                            op=ALU.add), deps=[m])
                    s.barrier()
        for f in range(16):
            d1 = s.dma("sync", rt[:, :], y1[f * 128:(f + 1) * 128, h0:h0 + 1024])
            v1 = s.add("vector", lambda eng, f=f: eng.tensor_tensor(
                out=acc[f][:, :], in0=rt[:, :], in1=acc[f][:, :], op=ALU.add),
                deps=[d1])
            s.barrier()
            d2 = s.dma("sync", rt[:, :], at[f * 128:(f + 1) * 128, h0:h0 + 1024])
            v2 = s.add("vector", lambda eng, f=f: eng.tensor_tensor(
                out=acc[f][:, :], in0=rt[:, :], in1=acc[f][:, :], op=ALU.add),
                deps=[d2, v1])
            s.dma("gpsimd", y2[f * 128:(f + 1) * 128, h0:h0 + 1024],
                  acc[f][:, :], deps=[v2])
            s.barrier()
    s.finish()
    s.build(nc)
    st.close()
    return nc


def kernel(**inp):
    ident = np.eye(128, dtype=BF)

    ncW = build_W()
    shard = lambda a, i, n=8: np.ascontiguousarray(
        a[i * (a.shape[0] // n):(i + 1) * (a.shape[0] // n)])
    wq1T = np.ascontiguousarray(inp["L1_attn_wq"].T)
    wk1T = np.ascontiguousarray(inp["L1_attn_wk"].T)
    wv1T = np.ascontiguousarray(inp["L1_attn_wv"].T)
    wo1T = np.ascontiguousarray(inp["L1_attn_wo"].T)
    in_maps = []
    for c in range(8):
        m = {
            "in_wg0": shard(inp["L0_ffn_w_gate"], c),
            "in_wu0": shard(inp["L0_ffn_w_up"], c),
            "in_wdn0": shard(inp["L0_ffn_w_down"], c),
            "in_wdk": shard(inp["L0_attn_Wd_k"], c),
            "in_wdv": shard(inp["L0_attn_Wd_v"], c),
            "in_wuk": shard(inp["L0_attn_Wu_k"], c),
            "in_wuv": shard(inp["L0_attn_Wu_v"], c),
            "in_wq0": shard(inp["L0_attn_Wq"], c),
            "in_wo0": shard(inp["L0_attn_Wo"], c),
            "in_wq1": shard(wq1T, c), "in_wk1": shard(wk1T, c),
            "in_wv1": shard(wv1T, c), "in_wo1": shard(wo1T, c),
            "in_g1": shard(inp["L1_ffn_gate_W_packed"], c),
            "in_g1s": shard(inp["L1_ffn_gate_scales"], c),
            "in_u1": shard(inp["L1_ffn_up_W_packed"], c),
            "in_u1s": shard(inp["L1_ffn_up_scales"], c),
            "in_d1": shard(inp["L1_ffn_down_W_packed"], c),
            "in_d1s": shard(inp["L1_ffn_down_scales"], c),
            "relp": np.ascontiguousarray(inp["L1_attn_rel_pos"]),
        }
        in_maps.append(m)
    resW = run_bass_kernel_spmd(ncW, in_maps, core_ids=list(range(8)))
    rW = resW.results
    cat = lambda key: np.concatenate([rW[c]["out_" + key] for c in range(8)], axis=0)
    wg0 = cat("wg0"); wu0 = cat("wu0"); wdn0 = cat("wdn0")
    wdk = cat("wdk"); wdv = cat("wdv"); wuk = cat("wuk"); wuv = cat("wuv")
    wq0 = cat("wq0"); wo0 = cat("wo0")
    wq1 = cat("wq1"); wk1 = cat("wk1"); wv1 = cat("wv1"); wo1 = cat("wo1")
    mbig = rW[0]["mbig"]

    def interleave(lo, hi):
        n, m = lo.shape
        out = np.empty((n, 2 * m), dtype=lo.dtype)
        out[:, 0::2] = lo
        out[:, 1::2] = hi
        return out
    WgT = np.ascontiguousarray(interleave(cat("g1lo"), cat("g1hi")).T)
    WuT = np.ascontiguousarray(interleave(cat("u1lo"), cat("u1hi")).T)
    WdT = np.ascontiguousarray(interleave(cat("d1lo"), cat("d1hi")).T)
    mbigR = np.ascontiguousarray(np.flip(mbig))

    x = inp["L0_attn_x"]
    xTfull = np.ascontiguousarray(x.reshape(16384, 2048).T)
    ncA = build_A()
    in_maps = []
    for c in range(8):
        in_maps.append({
            "xT": np.ascontiguousarray(xTfull[:, c * 2048:(c + 1) * 2048]),
            "wdk": wdk, "wdv": wdv, "wuk": wuk, "wuv": wuv,
            "wq0": wq0, "wo0": wo0,
            "bdk": inp["L0_attn_bd_k"], "bdv": inp["L0_attn_bd_v"],
            "buk": inp["L0_attn_bu_k"], "buv": inp["L0_attn_bu_v"],
            "bq": inp["L0_attn_bq"], "bo": inp["L0_attn_bo"],
            "wg": wg0, "wu": wu0, "wdn": wdn0,
            "sg": inp["L0_ffn_s_gate"], "su": inp["L0_ffn_s_up"],
            "sd": inp["L0_ffn_s_down"], "ident": ident,
        })
    resA = run_bass_kernel_spmd(ncA, in_maps, core_ids=list(range(8)))
    rA = resA.results
    y1 = [rA[c]["y1T"] for c in range(8)]
    xn1 = [rA[c]["xn1T"] for c in range(8)]

    ncB = build_B()
    in_maps = []
    for c in range(8):
        b = c // 2
        qbase = (c % 2) * 2048
        m2 = np.stack([mbig[qbase + 127 - p: qbase + 127 - p + 6016]
                       for p in range(128)], axis=0)
        in_maps.append({
            "xq": xn1[c], "xlo": xn1[2 * b], "xhi": xn1[2 * b + 1],
            "wq1": wq1, "wk1": wk1, "wv1": wv1, "wo1": wo1,
            "m2": np.ascontiguousarray(m2),
        })
    resB = run_bass_kernel_spmd(ncB, in_maps, core_ids=list(range(8)))
    rB = resB.results
    attn1 = [rB[c]["aT"] for c in range(8)]

    ncC = build_C()
    in_maps = []
    for c in range(8):
        in_maps.append({
            "xn": xn1[c], "y1": y1[c], "at": attn1[c],
            "wgT": WgT, "wuT": WuT, "wdT": WdT,
        })
    resC = run_bass_kernel_spmd(ncC, in_maps, core_ids=list(range(8)))
    rC = resC.results
    full = np.concatenate([rC[c]["y2"] for c in range(8)], axis=1)
    return np.ascontiguousarray(full.T).reshape(4, 4096, 2048).astype(np.float32)
